# version 04
# baseline (speedup 1.0000x reference)
import math
import numpy as np
import ml_dtypes
from contextlib import ExitStack

import concourse.bass as bass
import concourse.mybir as mybir
from concourse.bass_utils import run_bass_kernel_spmd

F32 = mybir.dt.float32
BF16 = mybir.dt.bfloat16
I32 = mybir.dt.int32
ALU = mybir.AluOpType
ACT = mybir.ActivationFunctionType
AX = mybir.AxisListType
BF16NP = ml_dtypes.bfloat16

D = 2048
H = 16
DK = 128
TB = 4096
TOWN = 2048
IREAL = 6464
IP = 6656
NCD = 16
NCI = 52
NGA = 8
NGO = 4
RSQDK = 1.0 / math.sqrt(DK)
CLIP = 1.4999
NUMEL = [D * D] * 4 + [IREAL * D] * 2 + [D * IREAL]


class Ctr:

    def __init__(self, sem):
        self.sem = sem
        self.n = 0

    def inc(self, inst, k=1):
        inst.then_inc(self.sem, k)
        self.n += k
        return self.n



def build_wsum():
    nc = bass.Bass("TRN2", target_bir_lowering=False)
    srcA = nc.dram_tensor("wsrcA", [21, 128, D], F32, kind="ExternalInput").ap()
    srcB = nc.dram_tensor("wsrcB", [2, 128, IP], F32, kind="ExternalInput").ap()
    wacc = nc.dram_tensor("wacc", [23, 8], F32, kind="ExternalInput").ap()
    eyef = nc.dram_tensor("eyef", [128, 128], F32, kind="ExternalInput").ap()
    onescf = nc.dram_tensor("onescf", [128, 1], F32, kind="ExternalInput").ap()
    psum8 = nc.dram_tensor("psum8", [1, 8], F32, kind="ExternalOutput").ap()

    with ExitStack() as ctx:
        E = ctx.enter_context
        sb = E(nc.sbuf_tensor("sb", [128, IP], F32))
        swacc = E(nc.sbuf_tensor("swacc", [23, 8], F32))
        seye = E(nc.sbuf_tensor("seye", [128, 128], F32))
        sones = E(nc.sbuf_tensor("sones", [128, 1], F32))
        scs = E(nc.sbuf_tensor("scs", [128, 1], F32))
        sprow = E(nc.sbuf_tensor("sprow", [1, 32], F32))
        spcol = E(nc.sbuf_tensor("spcol", [23, 1], F32))
        sout = E(nc.sbuf_tensor("sout", [1, 8], F32))
        pt = E(nc.psum_tensor("pt", [128, 32], F32))
        ld = Ctr(E(nc.semaphore("ld")))
        ve = Ctr(E(nc.semaphore("ve")))
        te = Ctr(E(nc.semaphore("te")))
        se = Ctr(E(nc.semaphore("se")))
        st = Ctr(E(nc.semaphore("st")))
        blk = E(nc.Block())

        for i in range(23):
            @blk.sync
            def _(e, i=i):
                if i == 0:
                    for dst, src in [(swacc, wacc), (seye, eyef), (sones, onescf)]:
                        ld.inc(e.dma_start(out=dst[:], in_=src), 16)
                e.wait_ge(ve.sem, ve.n)
                if i < 21:
                    ld.inc(e.dma_start(out=sb[:, 0:D], in_=srcA[i]), 16)
                else:
                    ld.inc(e.dma_start(out=sb[:], in_=srcB[i - 21]), 16)

            @blk.vector
            def _(e, i=i):
                w = D if i < 21 else IP
                e.wait_ge(ld.sem, ld.n)
                e.wait_ge(te.sem, te.n)
                ve.inc(e.reduce_sum(scs[:], sb[:, 0:w], AX.X,
                                    apply_absolute_value=True))

            @blk.tensor
            def _(e, i=i):
                e.wait_ge(ve.sem, ve.n)
                e.wait_ge(se.sem, se.n)
                te.inc(e.matmul(pt[0:1, i:i + 1], scs[:], sones[:],
                                start=True, stop=True))

            @blk.scalar
            def _(e, i=i):
                e.wait_ge(te.sem, te.n)
                se.inc(e.copy(sprow[0:1, i:i + 1], pt[0:1, i:i + 1]))

        @blk.tensor
        def _(e):
            e.wait_ge(se.sem, se.n)
            te.inc(e.transpose(pt[0:32, 0:1], sprow[0:1, 0:32], seye[0:1, 0:1]))

        @blk.vector
        def _(e):
            e.wait_ge(te.sem, te.n)
            ve.inc(e.tensor_copy(spcol[:], pt[0:23, 0:1]))

        @blk.tensor
        def _(e):
            e.wait_ge(ve.sem, ve.n)
            te.inc(e.matmul(pt[0:1, 0:8], spcol[:], swacc[:], start=True, stop=True))

        @blk.scalar
        def _(e):
            e.wait_ge(te.sem, te.n)
            se.inc(e.copy(sout[:], pt[0:1, 0:8]))

        @blk.gpsimd
        def _(e):
            e.wait_ge(se.sem, se.n)
            st.inc(e.dma_start(out=psum8, in_=sout[:]), 16)
            e.wait_ge(st.sem, st.n)
    return nc



def build_wquant():
    nc = bass.Bass("TRN2", target_bir_lowering=False)
    srcA = nc.dram_tensor("wsrcA", [21, 128, D], F32, kind="ExternalInput").ap()
    srcB = nc.dram_tensor("wsrcB", [2, 128, IP], F32, kind="ExternalInput").ap()
    wacct = nc.dram_tensor("wacct", [8, 23], F32, kind="ExternalInput").ap()
    wsums = nc.dram_tensor("wsums", [8, 8], F32, kind="ExternalInput").ap()
    ninv = nc.dram_tensor("ninv", [8, 1], F32, kind="ExternalInput").ap()
    eyeb = nc.dram_tensor("eyeb", [128, 128], BF16, kind="ExternalInput").ap()
    onescf = nc.dram_tensor("onescf", [128, 1], F32, kind="ExternalInput").ap()
    onesrf = nc.dram_tensor("onesrf", [1, 128], F32, kind="ExternalInput").ap()
    oshA = nc.dram_tensor("oshardA", [21, 128, NCD * 128], BF16,
                          kind="ExternalOutput").ap()
    oshB = nc.dram_tensor("oshardB", [2, 128, NCI * 128], BF16,
                          kind="ExternalOutput").ap()

    with ExitStack() as ctx:
        E = ctx.enter_context
        sb = E(nc.sbuf_tensor("sb", [128, IP], F32))
        st1 = E(nc.sbuf_tensor("st1", [128, IP], F32))
        srnd = E(nc.sbuf_tensor("srnd", [128, IP], F32))
        stern = E(nc.sbuf_tensor("stern", [128, IP], BF16))
        slab = E(nc.sbuf_tensor("slab", [128, NCI * 128], BF16))
        swacct = E(nc.sbuf_tensor("swacct", [8, 23], F32))
        swsums = E(nc.sbuf_tensor("swsums", [8, 8], F32))
        sninv = E(nc.sbuf_tensor("sninv", [8, 1], F32))
        seyeb = E(nc.sbuf_tensor("seyeb", [128, 128], BF16))
        sonesc = E(nc.sbuf_tensor("sonesc", [128, 1], F32))
        sonesr = E(nc.sbuf_tensor("sonesr", [1, 128], F32))
        ssw = E(nc.sbuf_tensor("ssw", [8, 1], F32))
        ssrow = E(nc.sbuf_tensor("ssrow", [1, 23], F32))
        sbc = E(nc.sbuf_tensor("sbc", [128, 1], F32))
        pt = E(nc.psum_tensor("pt", [128, 32], F32))
        tb = E(nc.psum_tensor("tb", [128, 1024], BF16))
        ld = Ctr(E(nc.semaphore("ld")))
        ve = Ctr(E(nc.semaphore("ve")))
        te = Ctr(E(nc.semaphore("te")))
        se = Ctr(E(nc.semaphore("se")))
        st = Ctr(E(nc.semaphore("st")))
        blk = E(nc.Block())

        @blk.sync
        def _(e):
            for dst, src in [(swacct, wacct), (swsums, wsums), (sninv, ninv),
                             (seyeb, eyeb), (sonesc, onescf), (sonesr, onesrf)]:
                ld.inc(e.dma_start(out=dst[:], in_=src), 16)

        @blk.tensor
        def _(e):
            e.wait_ge(ld.sem, ld.n)
            te.inc(e.matmul(pt[0:8, 0:1], swsums[:], sonesc[0:8, 0:1],
                            start=True, stop=True))

        @blk.vector
        def _(e):
            e.wait_ge(te.sem, te.n)
            ve.inc(e.tensor_tensor(ssw[:], pt[0:8, 0:1], sninv[:], ALU.mult))
            e.wait_ge(ve.sem, ve.n)
            ve.inc(e.tensor_scalar(ssw[:], ssw[:], 1e-5, None, ALU.max))

        @blk.scalar
        def _(e):
            e.wait_ge(ve.sem, ve.n)
            e.activation(ssw[:], ssw[:], ACT.Ln)
            se.inc(e.activation(ssw[:], ssw[:], ACT.Exp, scale=-1.0))

        @blk.tensor
        def _(e):
            e.wait_ge(se.sem, se.n)
            te.inc(e.matmul(pt[0:1, 0:23], ssw[:], swacct[:], start=True, stop=True))

        @blk.vector
        def _(e):
            e.wait_ge(te.sem, te.n)
            ve.inc(e.tensor_copy(ssrow[:], pt[0:1, 0:23]))

        for i in range(23):
            w = D if i < 21 else IP
            ncc = NCD if i < 21 else NCI

            @blk.sync
            def _(e, i=i):
                e.wait_ge(st.sem, st.n)
                e.wait_ge(ve.sem, ve.n)
                if i < 21:
                    ld.inc(e.dma_start(out=sb[:, 0:D], in_=srcA[i]), 16)
                else:
                    ld.inc(e.dma_start(out=sb[:], in_=srcB[i - 21]), 16)

            @blk.tensor
            def _(e, i=i):
                e.wait_ge(ve.sem, ve.n)
                te.inc(e.matmul(pt[:, 0:1], sonesr[:], ssrow[0:1, i:i + 1],
                                start=True, stop=True))

            @blk.vector
            def _(e, i=i, w=w):
                e.wait_ge(te.sem, te.n)
                ve.inc(e.tensor_copy(sbc[:], pt[:, 0:1]))
                e.wait_ge(ld.sem, ld.n)
                e.wait_ge(st.sem, st.n)
                e.wait_ge(ve.sem, ve.n)
                e.tensor_scalar(st1[:, 0:w], sb[:, 0:w], sbc[:, 0:1], CLIP,
                                ALU.mult, ALU.min)
                e.tensor_scalar(srnd[:, 0:w].bitcast(I32), st1[:, 0:w], -CLIP,
                                None, ALU.max)
                ve.inc(e.tensor_copy(stern[:, 0:w], srnd[:, 0:w].bitcast(I32)))

            nbanks = (ncc + 7) // 8
            for b in range(nbanks):
                hi = min(8, ncc - b * 8)

                @blk.tensor
                def _(e, b=b, hi=hi):
                    e.wait_ge(ve.sem, ve.n)
                    for q in range(hi):
                        cc = b * 8 + q
                        te.inc(e.transpose(tb[:, q * 128:q * 128 + 128],
                                           stern[:, cc * 128:cc * 128 + 128],
                                           seyeb[:]))

                @blk.vector
                def _(e, b=b, hi=hi):
                    e.wait_ge(te.sem, te.n)
                    ve.inc(e.tensor_copy(
                        slab[:, b * 1024:b * 1024 + hi * 128], tb[:, 0:hi * 128]))

            @blk.gpsimd
            def _(e, i=i):
                e.wait_ge(ve.sem, ve.n)
                if i < 21:
                    st.inc(e.dma_start(out=oshA[i], in_=slab[:, 0:NCD * 128]), 16)
                else:
                    st.inc(e.dma_start(out=oshB[i - 21], in_=slab[:]), 16)

        @blk.gpsimd
        def _(e):
            e.wait_ge(st.sem, st.n)
    return nc



def build_main():
    nc = bass.Bass("TRN2", target_bir_lowering=False)
    dram = lambda n, s, dt, k: nc.dram_tensor(n, s, dt, kind=k).ap()

    x_in = dram("x", [TB, D], F32, "ExternalInput")
    twq = dram("twq", [16, 128, NCD * 128], BF16, "ExternalInput")
    twk = dram("twk", [16, 128, NCD * 128], BF16, "ExternalInput")
    twv = dram("twv", [4, 128, NCD * 512], BF16, "ExternalInput")
    two = dram("two", [4, 128, NCD * 512], BF16, "ExternalInput")
    twg = dram("twg", [26, 128, NCD * 256], BF16, "ExternalInput")
    twu = dram("twu", [26, 128, NCD * 256], BF16, "ExternalInput")
    twd = dram("twd", [NCI, 128, 2048], BF16, "ExternalInput")
    wsums = dram("wsums", [8, 8], F32, "ExternalInput")
    ninvr = dram("ninvr", [1, 8], F32, "ExternalInput")
    g_in = {p: dram("g" + p, [D], F32, "ExternalInput") for p in "qkvo"}
    eyeb = dram("eyeb", [128, 128], BF16, "ExternalInput")
    eyef = dram("eyef", [128, 128], F32, "ExternalInput")
    onescf = dram("onescf", [128, 1], F32, "ExternalInput")
    onesrf = dram("onesrf", [1, 128], F32, "ExternalInput")
    ehs8 = dram("ehs8", [128, 8 * 128], BF16, "ExternalInput")
    out = dram("out", [TOWN, D], F32, "ExternalOutput")

    qscr = dram("qscr", [NGO, H, 128, 512], BF16, "Internal")
    kscr = dram("kscr", [H, 128, TB], BF16, "Internal")
    vscr = dram("vscr", [H, 128, 32, 128], BF16, "Internal")

    ctx = ExitStack()
    E = ctx.enter_context
    with ctx:
        gbo = E(nc.sbuf_tensor("gbo", [128, D], BF16))
        cwb = E(nc.sbuf_tensor("cwb", [128, 8], F32))
        seyeb = E(nc.sbuf_tensor("seyeb", [128, 128], BF16))
        seyef = E(nc.sbuf_tensor("seyef", [128, 128], F32))
        sonescf = E(nc.sbuf_tensor("sonescf", [128, 1], F32))
        sonesrf = E(nc.sbuf_tensor("sonesrf", [1, 128], F32))
        sehs = E(nc.sbuf_tensor("sehs", [128, 8 * 128], BF16))
        sconst = E(nc.sbuf_tensor("sconst", [128, 4], F32))
        km2row = E(nc.sbuf_tensor("km2row", [1, 16], F32))
        srow8 = E(nc.sbuf_tensor("srow8", [1, 8], F32))
        sninvr = E(nc.sbuf_tensor("sninvr", [1, 8], F32))
        swsums = E(nc.sbuf_tensor("swsums", [8, 8], F32))
        norm32 = E(nc.sbuf_tensor("norm32", [32, 32], F32))
        normT = E(nc.sbuf_tensor("normT", [32, 32], F32))
        qrow = E(nc.sbuf_tensor("qrow", [1, 16], F32))
        xn = E(nc.sbuf_tensor("xn", [128, D], F32))
        ybuf = E(nc.sbuf_tensor("ybuf", [128, D], F32))
        rnd32 = E(nc.sbuf_tensor("rnd32", [128, D], F32))
        xqtm = E(nc.sbuf_tensor("xqtm", [128, D], BF16))
        bnst = E(nc.sbuf_tensor("bnst", [128, 24], F32))
        mv = E(nc.sbuf_tensor("mv", [128, 2], F32))
        tt8 = E(nc.sbuf_tensor("tt8", [128, 8], F32))
        alj = E(nc.sbuf_tensor("alj", [128, 12], F32))

        psA = E(nc.psum_tensor("psA", [128, 512], F32))
        psB = E(nc.psum_tensor("psB", [128, 512], F32))
        psC = E(nc.psum_tensor("psC", [128, 512], F32))
        psD = E(nc.psum_tensor("psD", [128, 512], F32))
        psE = E(nc.psum_tensor("psE", [128, 512], F32))
        psF = E(nc.psum_tensor("psF", [128, 512], F32))

        ld = Ctr(E(nc.semaphore("ld")))
        gd = Ctr(E(nc.semaphore("gd")))
        te = Ctr(E(nc.semaphore("te")))
        ve = Ctr(E(nc.semaphore("ve")))
        se = Ctr(E(nc.semaphore("se")))
        blk = E(nc.Block())

        def barrier():
            cs = [ld, gd, te, ve, se]
            vals = [c.n for c in cs]
            for emit, own in [(blk.sync, ld), (blk.tensor, te), (blk.vector, ve),
                              (blk.scalar, se), (blk.gpsimd, gd)]:
                def f(e, own=own, vals=tuple(vals)):
                    for c, v in zip(cs, vals):
                        if c is not own:
                            e.wait_ge(c.sem, v)
                emit(f)

        def emit_ln(src):
            @blk.vector
            def _(e):
                e.wait_ge(se.sem, se.n)
                for q in range(4):
                    ins = e.bn_stats(bnst[:, q * 6:q * 6 + 6],
                                     src[:, q * 512:q * 512 + 512])
                ve.inc(ins)
                e.wait_ge(ve.sem, ve.n)
                ve.inc(e.bn_aggr(mv[:], bnst[:]))

            @blk.scalar
            def _(e):
                e.wait_ge(ve.sem, ve.n)
                e.activation(tt8[:, 0:1], mv[:, 1:2], ACT.Ln, bias=sconst[:, 0:1])
                se.inc(e.activation(tt8[:, 0:1], tt8[:, 0:1], ACT.Exp, scale=-0.5))

            @blk.vector
            def _(e):
                e.wait_ge(se.sem, se.n)
                e.scalar_tensor_tensor(tt8[:, 1:2], mv[:, 0:1], -1.0,
                                       tt8[:, 0:1], ALU.mult, ALU.mult)
                ve.inc(e.tensor_tensor(tt8[:, 3:4], tt8[:, 0:1], tt8[:, 0:1],
                                       ALU.mult))
                e.wait_ge(ve.sem, ve.n)
                ve.inc(e.tensor_tensor(tt8[:, 3:4], mv[:, 1:2], tt8[:, 3:4],
                                       ALU.mult))

            @blk.scalar
            def _(e):
                e.wait_ge(ve.sem, ve.n)
                e.activation(xn[:], src[:], ACT.Identity, bias=tt8[:, 1:2],
                             scale=tt8[:, 0:1])
                e.activation(tt8[:, 2:3], tt8[:, 3:4], ACT.Ln, bias=sconst[:, 1:2])
                se.inc(e.activation(tt8[:, 2:3], tt8[:, 2:3], ACT.Exp, scale=-0.5))

        def emit_rms(src):
            @blk.vector
            def _(e):
                e.wait_ge(se.sem, se.n)
                for q in range(4):
                    ins = e.bn_stats(bnst[:, q * 6:q * 6 + 6],
                                     src[:, q * 512:q * 512 + 512])
                ve.inc(ins)
                e.wait_ge(ve.sem, ve.n)
                ve.inc(e.bn_aggr(mv[:], bnst[:]))
                e.wait_ge(ve.sem, ve.n)
                ve.inc(e.tensor_tensor(tt8[:, 3:4], mv[:, 0:1], mv[:, 0:1],
                                       ALU.mult))
                e.wait_ge(ve.sem, ve.n)
                ve.inc(e.tensor_tensor(tt8[:, 3:4], mv[:, 1:2], tt8[:, 3:4],
                                       ALU.add))

            @blk.scalar
            def _(e):
                e.wait_ge(ve.sem, ve.n)
                e.activation(tt8[:, 2:3], tt8[:, 3:4], ACT.Ln, bias=sconst[:, 1:2])
                se.inc(e.activation(tt8[:, 2:3], tt8[:, 2:3], ACT.Exp, scale=-0.5))

        def emit_quant(src, w, gain, use_rms, wcol, acol, extra_acols=()):
            yy = src if gain is None else ybuf

            @blk.vector
            def _(e):
                e.wait_ge(se.sem, se.n)
                e.wait_ge(te.sem, te.n)
                if gain is not None:
                    e.tensor_tensor(ybuf[:, 0:w], src[:, 0:w], gain[:, 0:w],
                                    ALU.mult)
                ve.inc(e.reduce_max(tt8[:, 4:5], yy[:, 0:w], AX.X,
                                    apply_absolute_value=True))
                e.wait_ge(ve.sem, ve.n)
                if use_rms:
                    ve.inc(e.tensor_tensor(tt8[:, 4:5], tt8[:, 4:5],
                                           tt8[:, 2:3], ALU.mult))
                    e.wait_ge(ve.sem, ve.n)
                ve.inc(e.tensor_scalar(tt8[:, 4:5], tt8[:, 4:5], 1e-5, None,
                                       ALU.max))

            @blk.scalar
            def _(e):
                e.wait_ge(ve.sem, ve.n)
                e.activation(tt8[:, 5:6], tt8[:, 4:5], ACT.Ln)
                se.inc(e.activation(tt8[:, 5:6], tt8[:, 5:6], ACT.Exp,
                                    scale=-1.0))

            @blk.vector
            def _(e):
                e.wait_ge(se.sem, se.n)
                if use_rms:
                    ve.inc(e.tensor_tensor(tt8[:, 6:7], tt8[:, 5:6],
                                           tt8[:, 2:3], ALU.mult))
                else:
                    ve.inc(e.tensor_copy(tt8[:, 6:7], tt8[:, 5:6]))
                e.wait_ge(ve.sem, ve.n)
                nch = (w + D - 1) // D
                for q in range(nch):
                    lo, hi = q * D, min(w, (q + 1) * D)
                    e.tensor_scalar(rnd32[:, 0:hi - lo].bitcast(I32),
                                    yy[:, lo:hi], tt8[:, 6:7], 127.0,
                                    ALU.mult, ALU.mult)
                    e.tensor_copy(xqtm[:, lo:hi], rnd32[:, 0:hi - lo].bitcast(I32))
                last = e.tensor_tensor(alj[:, acol:acol + 1], tt8[:, 4:5],
                                       cwb[:, wcol:wcol + 1], ALU.mult)
                for (wc2, ac2) in extra_acols:
                    last = e.tensor_tensor(alj[:, ac2:ac2 + 1], tt8[:, 4:5],
                                           cwb[:, wc2:wc2 + 1], ALU.mult)
                ve.inc(last)

        def emit_transposes(tbs, src_tm, ncc, dstap_fn):
            nb = (ncc + 7) // 8
            for b in range(nb):
                hi = min(8, ncc - b * 8)

                @blk.tensor
                def _(e, b=b, hi=hi):
                    e.wait_ge(ve.sem, ve.n)
                    for q in range(hi):
                        cc = b * 8 + q
                        te.inc(e.transpose(
                            tbs[b % 2][:, q * 128:q * 128 + 128],
                            src_tm[:, cc * 128:cc * 128 + 128], seyeb[:]))

                @blk.vector
                def _(e, b=b, hi=hi):
                    e.wait_ge(te.sem, te.n)
                    ve.inc(e.tensor_copy(dstap_fn(b, hi),
                                         tbs[b % 2][:, 0:hi * 128]))

        def fm4(buf):
            return buf[:].rearrange("p (cc j f) -> p cc j f", j=4, f=128)

        def fm512(buf):
            return buf[:].rearrange("p (cc r) -> p cc r", r=512)

        @blk.vector
        def _(e):
            e.memset(sconst[:, 0:1], 1e-5)
            e.memset(sconst[:, 1:2], 1e-6)
            e.memset(sconst[:, 2:3], 1e-30)
            e.memset(sconst[:, 3:4], 0.0)
            e.memset(norm32[:], 0.0)
            ve.inc(e.memset(alj[:], 0.0))

        @blk.sync
        def _(e):
            for dst, src in [(seyeb, eyeb), (seyef, eyef), (sonescf, onescf),
                             (sonesrf, onesrf), (sehs, ehs8),
                             (sninvr, ninvr), (swsums, wsums)]:
                ld.inc(e.dma_start(out=dst[:], in_=src), 16)

        @blk.tensor
        def _(e):
            e.wait_ge(ld.sem, 7 * 16)
            te.inc(e.matmul(psF[0:1, 0:8], sonescf[0:8, 0:1], swsums[:],
                            start=True, stop=True))

        @blk.vector
        def _(e):
            e.wait_ge(te.sem, te.n)
            ve.inc(e.tensor_tensor(srow8[:], psF[0:1, 0:8], sninvr[:],
                                   ALU.mult))
            e.wait_ge(ve.sem, ve.n)
            ve.inc(e.tensor_scalar(srow8[:], srow8[:], 1e-5, None, ALU.max))
            e.wait_ge(ve.sem, ve.n)
            ve.inc(e.tensor_scalar(srow8[:], srow8[:], 1.0 / 127.0, None,
                                   ALU.mult))

        @blk.tensor
        def _(e):
            e.wait_ge(ve.sem, ve.n)
            te.inc(e.matmul(psF[:, 0:8], sonesrf[:], srow8[:],
                            start=True, stop=True))

        @blk.vector
        def _(e):
            e.wait_ge(te.sem, te.n)
            ve.inc(e.tensor_copy(cwb[:], psF[:, 0:8]))
            e.wait_ge(ve.sem, ve.n)
            ve.inc(e.tensor_scalar(cwb[:, 0:1], cwb[:, 0:1], RSQDK, None,
                                   ALU.mult))

        barrier()

        with ExitStack() as actx:
            aE = actx.enter_context
            gb = {"o": gbo}
            for p in "qkv":
                gb[p] = aE(nc.sbuf_tensor("gb" + p, [128, D], BF16))
            xbuf = [aE(nc.sbuf_tensor(f"xbuf{i}", [128, D], F32)) for i in range(2)]
            xqfm = {p: aE(nc.sbuf_tensor("xqfm" + p, [128, NCD * 512], BF16))
                    for p in "qkv"}
            wsl = [aE(nc.sbuf_tensor(f"wsl{i}", [128, NCD * 128], BF16))
                   for i in range(2)]
            vsl = [aE(nc.sbuf_tensor(f"vsl{i}", [128, NCD * 512], BF16))
                   for i in range(2)]
            qob = [aE(nc.sbuf_tensor(f"qob{i}", [128, 512], BF16)) for i in range(2)]
            vob = [aE(nc.sbuf_tensor(f"vob{i}", [128, 512], BF16)) for i in range(2)]
            abc = aE(nc.sbuf_tensor("abc", [128, 512], F32))
            arow = aE(nc.sbuf_tensor("arow", [1, 512], F32))
            tb0 = aE(nc.psum_tensor("tbA0", [128, 1024], BF16))
            tb1 = aE(nc.psum_tensor("tbA1", [128, 1024], BF16))
            tbs = [tb0, tb1]

            ld_g0 = ld.n
            ve_g0 = ve.n

            @blk.sync
            def _(e):
                for i, p in enumerate("qkvo"):
                    e.wait_ge(ve.sem, ve_g0 + i)
                    ld.inc(e.dma_start(
                        out=xbuf[0][:],
                        in_=g_in[p][None, :].partition_broadcast(128)), 16)

            @blk.vector
            def _(e):
                for i, p in enumerate("qkvo"):
                    e.wait_ge(ld.sem, ld_g0 + (i + 1) * 16)
                    ve.inc(e.tensor_copy(gb[p][:], xbuf[0][:]))

            barrier()

            for g in range(NGA):
                own = g < NGO
                projs = "qkv" if own else "kv"
                for j in range(4):
                    tt = g * 4 + j

                    @blk.sync
                    def _(e, j=j, tt=tt):
                        e.wait_ge(se.sem, se.n)
                        ld.inc(e.dma_start(out=xbuf[j % 2][:],
                                           in_=x_in[tt * 128:(tt + 1) * 128, :]),
                               16)

                    @blk.vector
                    def _(e):
                        e.wait_ge(ld.sem, ld.n)
                        ve.inc(e.engine_nop())

                    emit_ln(xbuf[j % 2])
                    for p in projs:
                        pi = "qkv".index(p)
                        emit_quant(xn, D, gb[p], True, "qkvo".index(p),
                                   pi * 4 + j)
                        emit_transposes(
                            tbs, xqtm, NCD,
                            lambda b, hi, p=p, j=j: fm4(xqfm[p])
                                [:, b * 8:b * 8 + hi, j, :])

                for p in ("qk" if own else "k"):
                    pi = "qkv".index(p)

                    @blk.tensor
                    def _(e, pi=pi):
                        e.wait_ge(ve.sem, ve.n)
                        for j in range(4):
                            te.inc(e.transpose(psF[0:1, j * 128:j * 128 + 128],
                                               alj[:, pi * 4 + j:pi * 4 + j + 1],
                                               seyef[:]))

                    @blk.vector
                    def _(e):
                        e.wait_ge(te.sem, te.n)
                        ve.inc(e.tensor_copy(arow[:], psF[0:1, 0:512]))

                    @blk.tensor
                    def _(e):
                        e.wait_ge(ve.sem, ve.n)
                        te.inc(e.matmul(psF[:, 0:512], sonesrf[:], arow[:],
                                        start=True, stop=True))

                    @blk.vector
                    def _(e):
                        e.wait_ge(te.sem, te.n)
                        ve.inc(e.tensor_copy(abc[:], psF[:, 0:512]))

                    tw = twq if p == "q" else twk
                    ld0, te0, ve0, gd0 = ld.n, te.n, ve.n, gd.n
                    ps = [psA, psB]

                    @blk.sync
                    def _(e, tw=tw, ld0=ld0, te0=te0):
                        for s in range(16):
                            e.wait_ge(te.sem, te0 + (s - 1) if s >= 2 else te0)
                            ld.inc(e.dma_start(out=wsl[s % 2][:], in_=tw[s]), 16)

                    @blk.tensor
                    def _(e, p=p, ld0=ld0, ve0=ve0):
                        for s in range(16):
                            e.wait_ge(ld.sem, ld0 + 16 * (s + 1))
                            e.wait_ge(ve.sem, ve0 + (s - 1) if s >= 2 else ve0)
                            for cc in range(NCD):
                                mm = e.matmul(
                                    ps[s % 2][:, 0:512],
                                    wsl[s % 2][:, cc * 128:cc * 128 + 128],
                                    fm512(xqfm[p])[:, cc, :],
                                    start=(cc == 0), stop=(cc == NCD - 1))
                            te.inc(mm)

                    @blk.vector
                    def _(e, te0=te0, gd0=gd0):
                        for s in range(16):
                            e.wait_ge(te.sem, te0 + s + 1)
                            e.wait_ge(gd.sem, gd0 + 16 * (s - 1) if s >= 2 else gd0)
                            ve.inc(e.tensor_tensor(qob[s % 2][:],
                                                   ps[s % 2][:, 0:512],
                                                   abc[:], ALU.mult))

                    @blk.gpsimd
                    def _(e, p=p, g=g, ve0=ve0):
                        for s in range(16):
                            e.wait_ge(ve.sem, ve0 + s + 1)
                            if p == "q":
                                dst = qscr[g, s]
                            else:
                                dst = kscr[s][:, g * 512:(g + 1) * 512]
                            gd.inc(e.dma_start(out=dst, in_=qob[s % 2][:]), 16)

                ld0, te0, se0, gd0 = ld.n, te.n, se.n, gd.n
                psv = [psC, psD]

                @blk.sync
                def _(e, ld0=ld0, te0=te0):
                    for ob in range(4):
                        if ob >= 2:
                            e.wait_ge(te.sem, te0 + (ob - 1) * 4)
                        ld.inc(e.dma_start(out=vsl[ob % 2][:], in_=twv[ob]), 16)

                @blk.tensor
                def _(e, ld0=ld0, se0=se0):
                    for ob in range(4):
                        e.wait_ge(ld.sem, ld0 + 16 * (ob + 1))
                        for j in range(4):
                            m = ob * 4 + j
                            if m >= 2:
                                e.wait_ge(se.sem, se0 + (m - 1))
                            for cc in range(NCD):
                                mm = e.matmul(
                                    psv[m % 2][:, 0:512],
                                    fm4(xqfm["v"])[:, cc, j, :],
                                    fm512(vsl[ob % 2])[:, cc, :],
                                    start=(cc == 0), stop=(cc == NCD - 1))
                            te.inc(mm)

                @blk.scalar
                def _(e, te0=te0, gd0=gd0):
                    for ob in range(4):
                        for j in range(4):
                            m = ob * 4 + j
                            e.wait_ge(te.sem, te0 + m + 1)
                            if m >= 2:
                                e.wait_ge(gd.sem, gd0 + 64 * (m - 1))
                            se.inc(e.activation(vob[m % 2][:],
                                                psv[m % 2][:, 0:512], ACT.Copy,
                                                scale=alj[:, 8 + j:8 + j + 1]))

                @blk.gpsimd
                def _(e, g=g, se0=se0):
                    for ob in range(4):
                        for j in range(4):
                            m = ob * 4 + j
                            tt = g * 4 + j
                            e.wait_ge(se.sem, se0 + m + 1)
                            for q in range(4):
                                gd.inc(e.dma_start(
                                    out=vscr[4 * ob + q][:, tt, :],
                                    in_=vob[m % 2][:, q * 128:q * 128 + 128]), 16)

                barrier()

        barrier()

        with ExitStack() as bctx:
            bE = bctx.enter_context
            kb0 = bE(nc.sbuf_tensor("kb0", [128, TB], BF16))
            k2 = bE(nc.sbuf_tensor("k2", [128, TB], BF16))
            for h in range(H):
                @blk.sync
                def _(e, h=h):
                    e.wait_ge(ve.sem, ve.n)
                    ld.inc(e.dma_start(out=kb0[:], in_=kscr[h]), 16)

                @blk.vector
                def _(e):
                    e.wait_ge(ld.sem, ld.n)
                    e.wait_ge(te.sem, te.n)
                    ve.inc(e.tensor_tensor(k2[:], kb0[:], kb0[:], ALU.mult))

                @blk.tensor
                def _(e):
                    e.wait_ge(ve.sem, ve.n)
                    for jj in range(8):
                        mm = e.matmul(psA[:, 0:512],
                                      sehs[:, jj * 128:jj * 128 + 128],
                                      k2[:, jj * 512:jj * 512 + 512],
                                      start=(jj == 0), stop=(jj == 7))
                    te.inc(mm)

                @blk.vector
                def _(e, h=h):
                    e.wait_ge(te.sem, te.n)
                    ve.inc(e.reduce_max(norm32[0:8, 0:1], psA[0:8, 0:512],
                                        AX.X))
                    e.wait_ge(ve.sem, ve.n)
                    ve.inc(e.transpose(normT[:], norm32[:]))
                    e.wait_ge(ve.sem, ve.n)
                    ve.inc(e.reduce_max(km2row[0:1, h:h + 1], normT[0:1, 0:8],
                                        AX.X))

        barrier()

        for g in range(NGO):
            with ExitStack() as gctx:
                gE = gctx.enter_context
                v1 = [gE(nc.sbuf_tensor(f"v1_{g}_{j}", [128, D], F32))
                      for j in range(4)]
                xqpfm = gE(nc.sbuf_tensor(f"xqpfm{g}", [128, NCD * 512], BF16))
                sC = ExitStack()
                ao = [sC.enter_context(nc.sbuf_tensor(f"ao{g}{j}", [128, D], F32))
                      for j in range(4)]
                sB = ExitStack()
                attnout = sB.enter_context(
                    nc.sbuf_tensor(f"attnout{g}", [128, H * 512], BF16))
                negbc = sB.enter_context(nc.sbuf_tensor(f"negbc{g}", [128, 16], F32))
                negrow = sB.enter_context(nc.sbuf_tensor(f"negrow{g}", [1, 16], F32))
                ssbA = sB.enter_context(nc.sbuf_tensor(f"ssbA{g}", [8, 512], F32))
                ssbB = sB.enter_context(nc.sbuf_tensor(f"ssbB{g}", [8, 512], F32))
                rec = sB.enter_context(nc.sbuf_tensor(f"rec{g}", [128, 64], F32))
                sA = ExitStack()
                qgrp = sA.enter_context(
                    nc.sbuf_tensor(f"qgrp{g}", [128, H * 512], BF16))
                kb = [sA.enter_context(nc.sbuf_tensor(f"kb{g}{i}", [128, TB], BF16))
                      for i in range(2)]
                vb = [sA.enter_context(nc.sbuf_tensor(f"vb{g}{i}", [128, TB], BF16))
                      for i in range(2)]
                pbuf = sA.enter_context(
                    nc.sbuf_tensor(f"pbuf{g}", [128, 1024], BF16))

                @blk.sync
                def _(e, g=g):
                    e.wait_ge(gd.sem, gd.n)
                    for j in range(4):
                        tt = g * 4 + j
                        ld.inc(e.dma_start(
                            out=v1[j][:], in_=x_in[tt * 128:(tt + 1) * 128, :]), 16)
                    ld.inc(e.dma_start(
                        out=qgrp[:].rearrange("p (h f) -> p h f", f=512),
                        in_=qscr[g].rearrange("h p f -> p h f")), 16)

                @blk.vector
                def _(e):
                    e.wait_ge(ld.sem, ld.n)
                    ve.inc(e.tensor_tensor(attnout[:], qgrp[:], qgrp[:],
                                           ALU.mult))

                @blk.tensor
                def _(e):
                    e.wait_ge(ve.sem, ve.n)
                    for h in range(H):
                        bank = psA if h < 8 else psB
                        mm = e.matmul(bank[:, 0:512],
                                      sehs[:, (h % 8) * 128:(h % 8) * 128 + 128],
                                      attnout[:, h * 512:h * 512 + 512],
                                      start=(h % 8 == 0), stop=(h % 8 == 7))
                        if h % 8 == 7:
                            te.inc(mm)

                @blk.vector
                def _(e):
                    e.wait_ge(te.sem, te.n)
                    ve.inc(e.reduce_max(norm32[0:8, 0:1], psA[0:8, 0:512],
                                        AX.X))
                    e.wait_ge(ve.sem, ve.n)
                    ve.inc(e.transpose(normT[:], norm32[:]))
                    e.wait_ge(ve.sem, ve.n)
                    e.tensor_copy(qrow[0:1, 0:8], normT[0:1, 0:8])
                    ve.inc(e.reduce_max(norm32[0:8, 0:1], psB[0:8, 0:512],
                                        AX.X))
                    e.wait_ge(ve.sem, ve.n)
                    ve.inc(e.transpose(normT[:], norm32[:]))
                    e.wait_ge(ve.sem, ve.n)
                    ve.inc(e.tensor_copy(qrow[0:1, 8:16], normT[0:1, 0:8]))
                    e.wait_ge(ve.sem, ve.n)
                    ve.inc(e.tensor_tensor(negrow[:], qrow[:],
                                           km2row[:], ALU.mult))

                @blk.scalar
                def _(e):
                    e.wait_ge(ve.sem, ve.n)
                    e.activation(negrow[:], negrow[:], ACT.Ln,
                                 bias=sconst[0:1, 2:3])
                    se.inc(e.activation(negrow[:], negrow[:], ACT.Exp, scale=0.5))

                @blk.vector
                def _(e):
                    e.wait_ge(se.sem, se.n)
                    ve.inc(e.tensor_scalar(negrow[:], negrow[:], -1.001, None,
                                           ALU.mult))

                @blk.tensor
                def _(e):
                    e.wait_ge(ve.sem, ve.n)
                    te.inc(e.matmul(psF[:, 0:16], sonesrf[:], negrow[:],
                                    start=True, stop=True))

                @blk.vector
                def _(e):
                    e.wait_ge(te.sem, te.n)
                    ve.inc(e.tensor_copy(negbc[:], psF[:, 0:16]))

                barrier()

                ld0, te0, se0, ve0 = ld.n, te.n, se.n, ve.n
                teh = lambda h: te0 + 64 * h

                @blk.sync
                def _(e, ld0=ld0, te0=te0):
                    for h in range(H):
                        if h >= 2:
                            e.wait_ge(te.sem, teh(h - 1))
                        ld.inc(e.dma_start(out=kb[h % 2][:], in_=kscr[h]), 16)
                        ld.inc(e.dma_start(
                            out=vb[h % 2][:],
                            in_=vscr[h].rearrange("p t f -> p (t f)")), 16)

                @blk.tensor
                def _(e, ld0=ld0, se0=se0, ve0=ve0):
                    for h in range(H):
                        e.wait_ge(ld.sem, ld0 + 32 * (h + 1))
                        e.wait_ge(se.sem, se0 + 32 * h)
                        e.wait_ge(ve.sem, ve0 + h)
                        for kc in range(2):
                            te.inc(e.matmul(
                                [psA, psB][kc][:, 0:512],
                                kb[h % 2][:, kc * 128:kc * 128 + 128],
                                qgrp[:, h * 512:h * 512 + 512],
                                start=True, stop=True))
                        for kc in range(32):
                            e.wait_ge(se.sem, se0 + 32 * h + kc + 1)
                            e.matmul(psC[:, 0:512],
                                     vb[h % 2][:, kc * 128:kc * 128 + 128],
                                     pbuf[:, (kc % 2) * 512:(kc % 2) * 512 + 512],
                                     start=(kc == 0), stop=(kc == 31))
                            bank = psD if h < 8 else psE
                            te.inc(e.matmul(
                                bank[:, 0:512],
                                sehs[:, (h % 8) * 128:(h % 8) * 128 + 128],
                                pbuf[:, (kc % 2) * 512:(kc % 2) * 512 + 512],
                                start=(h % 8 == 0 and kc == 0),
                                stop=(h % 8 == 7 and kc == 31)))
                            if kc + 2 < 32:
                                te.inc(e.matmul(
                                    [psA, psB][kc % 2][:, 0:512],
                                    kb[h % 2][:, (kc + 2) * 128:(kc + 2) * 128 + 128],
                                    qgrp[:, h * 512:h * 512 + 512],
                                    start=True, stop=True))

                @blk.scalar
                def _(e, te0=te0):
                    for h in range(H):
                        for kc in range(32):
                            if kc < 2:
                                tev = teh(h) + kc + 1
                            else:
                                tev = teh(h) + 2 * kc
                            e.wait_ge(te.sem, tev)
                            se.inc(e.activation(
                                pbuf[:, (kc % 2) * 512:(kc % 2) * 512 + 512],
                                [psA, psB][kc % 2][:, 0:512], ACT.Exp,
                                bias=negbc[:, h:h + 1]))

                @blk.vector
                def _(e, te0=te0):
                    for h in range(H):
                        e.wait_ge(te.sem, teh(h + 1))
                        ve.inc(e.tensor_copy(attnout[:, h * 512:h * 512 + 512],
                                             psC[:, 0:512]))

                barrier()
                sA.close()

                @blk.vector
                def _(e):
                    e.tensor_copy(ssbA[:], psD[0:8, 0:512])
                    ve.inc(e.tensor_copy(ssbB[:], psE[0:8, 0:512]))

                @blk.tensor
                def _(e):
                    e.wait_ge(ve.sem, ve.n)
                    for j in range(4):
                        e.transpose(psF[:, j * 16:j * 16 + 8],
                                    ssbA[0:8, j * 128:j * 128 + 128],
                                    seyef[0:8, 0:8])
                        te.inc(e.transpose(psF[:, j * 16 + 8:j * 16 + 16],
                                           ssbB[0:8, j * 128:j * 128 + 128],
                                           seyef[0:8, 0:8]))

                @blk.scalar
                def _(e):
                    e.wait_ge(te.sem, te.n)
                    e.activation(rec[:], psF[:, 0:64], ACT.Ln)
                    se.inc(e.activation(rec[:], rec[:], ACT.Exp, scale=-1.0))

                with nc.psum_tensor(f"tbB0{g}", [128, 1024], BF16) as tg0, \
                     nc.psum_tensor(f"tbB1{g}", [128, 1024], BF16) as tg1:
                    tgs = [tg0, tg1]
                    for j in range(4):
                        for hb in range(2):
                            @blk.tensor
                            def _(e, j=j, hb=hb):
                                e.wait_ge(ve.sem, ve.n)
                                for q in range(8):
                                    h = hb * 8 + q
                                    te.inc(e.transpose(
                                        tgs[hb][:, q * 128:q * 128 + 128],
                                        attnout[:, h * 512 + j * 128:
                                                h * 512 + j * 128 + 128],
                                        seyeb[:]))

                            @blk.vector
                            def _(e, j=j, hb=hb):
                                e.wait_ge(te.sem, te.n)
                                e.wait_ge(se.sem, se.n)
                                ins = None
                                for q in range(8):
                                    h = hb * 8 + q
                                    ins = e.tensor_scalar(
                                        ao[j][:, h * 128:h * 128 + 128],
                                        tgs[hb][:, q * 128:q * 128 + 128],
                                        rec[:, j * 16 + h:j * 16 + h + 1],
                                        None, ALU.mult)
                                ve.inc(ins)

                    sB.close()
                    for j in range(4):
                        emit_rms(ao[j])
                        emit_quant(ao[j], D, gbo, True, 3, j)
                        emit_transposes(
                            tgs, xqtm, NCD,
                            lambda b, hi, j=j: fm4(xqpfm)
                                [:, b * 8:b * 8 + hi, j, :])

                barrier()

                sD = ExitStack()
                osl = [sD.enter_context(
                    nc.sbuf_tensor(f"osl{g}{i}", [128, NCD * 512], BF16))
                    for i in range(2)]
                ld0, te0, ve0 = ld.n, te.n, ve.n
                pso = [psA, psB]

                @blk.sync
                def _(e, ld0=ld0, te0=te0):
                    for ob in range(4):
                        if ob >= 2:
                            e.wait_ge(te.sem, te0 + (ob - 1) * 4)
                        ld.inc(e.dma_start(out=osl[ob % 2][:], in_=two[ob]), 16)

                @blk.tensor
                def _(e, ld0=ld0, ve0=ve0):
                    for ob in range(4):
                        e.wait_ge(ld.sem, ld0 + 16 * (ob + 1))
                        for j in range(4):
                            m = ob * 4 + j
                            if m >= 2:
                                e.wait_ge(ve.sem, ve0 + (m - 1))
                            for cc in range(NCD):
                                mm = e.matmul(
                                    pso[m % 2][:, 0:512],
                                    fm4(xqpfm)[:, cc, j, :],
                                    fm512(osl[ob % 2])[:, cc, :],
                                    start=(cc == 0), stop=(cc == NCD - 1))
                            te.inc(mm)

                @blk.vector
                def _(e, te0=te0):
                    for ob in range(4):
                        for j in range(4):
                            m = ob * 4 + j
                            e.wait_ge(te.sem, te0 + m + 1)
                            ve.inc(e.scalar_tensor_tensor(
                                v1[j][:, ob * 512:ob * 512 + 512],
                                pso[m % 2][:, 0:512], alj[:, j:j + 1],
                                v1[j][:, ob * 512:ob * 512 + 512],
                                ALU.mult, ALU.add))

                barrier()
                sD.close()
                sC.close()

                sE = ExitStack()
                gsl = [sE.enter_context(
                    nc.sbuf_tensor(f"gsl{g}{i}", [128, NCD * 256], BF16))
                    for i in range(2)]
                usl = [sE.enter_context(
                    nc.sbuf_tensor(f"usl{g}{i}", [128, NCD * 256], BF16))
                    for i in range(2)]
                gu = [sE.enter_context(
                    nc.sbuf_tensor(f"gu{g}{j}", [128, IP], BF16))
                    for j in range(4)]
                gtmp = sE.enter_context(nc.sbuf_tensor(f"gtmp{g}", [128, 256], BF16))
                utmp = sE.enter_context(nc.sbuf_tensor(f"utmp{g}", [128, 256], BF16))
                guqfm = sE.enter_context(
                    nc.sbuf_tensor(f"guqfm{g}", [128, NCI * 256], BF16))
                dch = [sE.enter_context(
                    nc.sbuf_tensor(f"dch{g}{i}", [128, 2048], BF16))
                    for i in range(2)]
                otmp = [sE.enter_context(
                    nc.sbuf_tensor(f"otmp{g}{i}", [128, 512], F32))
                    for i in range(2)]

                with nc.psum_tensor(f"tbC0{g}", [128, 1024], BF16) as tc0, \
                     nc.psum_tensor(f"tbC1{g}", [128, 1024], BF16) as tc1:
                    tcs = [tc0, tc1]
                    for j in range(4):
                        emit_ln(v1[j])
                        emit_quant(xn, D, None, False, 4, j,
                                   extra_acols=[(5, 4 + j)])
                        emit_transposes(
                            tcs, xqtm, NCD,
                            lambda b, hi, j=j: fm4(xqpfm)
                                [:, b * 8:b * 8 + hi, j, :])

                    barrier()

                    ld0, te0, se0, ve0 = ld.n, te.n, se.n, ve.n
                    pg = [psA, psB]
                    pu = [psC, psD]

                    @blk.sync
                    def _(e, ld0=ld0, te0=te0):
                        for hb in range(26):
                            if hb >= 2:
                                e.wait_ge(te.sem, te0 + (hb - 1) * 4)
                            ld.inc(e.dma_start(out=gsl[hb % 2][:], in_=twg[hb]), 16)
                            ld.inc(e.dma_start(out=usl[hb % 2][:], in_=twu[hb]), 16)

                    @blk.tensor
                    def _(e, ld0=ld0, se0=se0):
                        for hb in range(26):
                            e.wait_ge(ld.sem, ld0 + 32 * (hb + 1))
                            for j in range(4):
                                m = hb * 4 + j
                                if m >= 2:
                                    e.wait_ge(se.sem, se0 + (m - 1))
                                for cc in range(NCD):
                                    xq = fm4(xqpfm)[:, cc, j, :]
                                    e.matmul(
                                        pg[m % 2][:, 0:256], xq,
                                        gsl[hb % 2][:].rearrange(
                                            "p (cc r) -> p cc r", r=256)[:, cc, :],
                                        start=(cc == 0), stop=(cc == NCD - 1))
                                    mm = e.matmul(
                                        pu[m % 2][:, 0:256], xq,
                                        usl[hb % 2][:].rearrange(
                                            "p (cc r) -> p cc r", r=256)[:, cc, :],
                                        start=(cc == 0), stop=(cc == NCD - 1))
                                te.inc(mm)

                    @blk.scalar
                    def _(e, te0=te0, ve0=ve0):
                        for hb in range(26):
                            for j in range(4):
                                m = hb * 4 + j
                                e.wait_ge(te.sem, te0 + m + 1)
                                if m >= 1:
                                    e.wait_ge(ve.sem, ve0 + m)
                                e.activation(gtmp[:], pg[m % 2][:, 0:256],
                                             ACT.Silu, scale=alj[:, j:j + 1])
                                se.inc(e.activation(utmp[:], pu[m % 2][:, 0:256],
                                                    ACT.Copy,
                                                    scale=alj[:, 4 + j:4 + j + 1]))

                    @blk.vector
                    def _(e, se0=se0):
                        for hb in range(26):
                            for j in range(4):
                                m = hb * 4 + j
                                e.wait_ge(se.sem, se0 + m + 1)
                                ve.inc(e.tensor_tensor(
                                    gu[j][:, hb * 256:hb * 256 + 256],
                                    gtmp[:], utmp[:], ALU.mult))

                    barrier()

                for pr in range(2):
                    with nc.psum_tensor(f"tbD0{g}{pr}", [128, 1024], BF16) as td0, \
                         nc.psum_tensor(f"tbD1{g}{pr}", [128, 1024], BF16) as td1:
                        tds = [td0, td1]
                        for j2 in range(2):
                            t = pr * 2 + j2

                            @blk.vector
                            def _(e, t=t):
                                e.wait_ge(se.sem, se.n)
                                e.wait_ge(te.sem, te.n)
                                ve.inc(e.reduce_max(
                                    tt8[:, 4:5], gu[t][:], AX.X,
                                    apply_absolute_value=True))
                                e.wait_ge(ve.sem, ve.n)
                                ve.inc(e.tensor_scalar(tt8[:, 4:5],
                                                       tt8[:, 4:5], 1e-5,
                                                       None, ALU.max))

                            @blk.scalar
                            def _(e):
                                e.wait_ge(ve.sem, ve.n)
                                e.activation(tt8[:, 6:7], tt8[:, 4:5], ACT.Ln)
                                se.inc(e.activation(tt8[:, 6:7], tt8[:, 6:7],
                                                    ACT.Exp, scale=-1.0))

                            @blk.vector
                            def _(e, t=t):
                                e.wait_ge(se.sem, se.n)
                                for q in range(4):
                                    lo = q * 1664
                                    e.tensor_scalar(
                                        rnd32[:, 0:1664].bitcast(I32),
                                        gu[t][:, lo:lo + 1664], tt8[:, 6:7],
                                        127.0, ALU.mult, ALU.mult)
                                    e.tensor_copy(gu[t][:, lo:lo + 1664],
                                                  rnd32[:, 0:1664].bitcast(I32))
                                ve.inc(e.tensor_tensor(
                                    alj[:, 8 + t:8 + t + 1], tt8[:, 4:5],
                                    cwb[:, 6:7], ALU.mult))

                            emit_transposes(
                                tds, gu[t], NCI,
                                lambda b, hi, j2=j2: guqfm[:].rearrange(
                                    "p (cc j f) -> p cc j f", j=2, f=128)
                                    [:, b * 8:b * 8 + hi, j2, :])

                        barrier()

                    if True:
                        with nc.psum_tensor(f"pdG{g}{pr}", [128, 512], F32) as psG, \
                             nc.psum_tensor(f"pdH{g}{pr}", [128, 512], F32) as psH:
                            pd = [psA, psB, psC, psD, psE, psF, psG, psH]
                            ld0, te0, ve0, gd0 = ld.n, te.n, ve.n, gd.n

                            @blk.sync
                            def _(e, ld0=ld0, te0=te0):
                                for ci in range(NCI):
                                    if ci >= 2:
                                        e.wait_ge(te.sem, te0 + (ci - 1))
                                    ld.inc(e.dma_start(out=dch[ci % 2][:],
                                                       in_=twd[ci]), 16)

                            @blk.tensor
                            def _(e, ld0=ld0):
                                for ci in range(NCI):
                                    e.wait_ge(ld.sem, ld0 + 16 * (ci + 1))
                                    for j2 in range(2):
                                        for ob in range(4):
                                            mm = e.matmul(
                                                pd[j2 * 4 + ob][:, 0:512],
                                                guqfm[:].rearrange(
                                                    "p (cc j f) -> p cc j f",
                                                    j=2, f=128)[:, ci, j2, :],
                                                dch[ci % 2][:, ob * 512:
                                                            ob * 512 + 512],
                                                start=(ci == 0),
                                                stop=(ci == NCI - 1))
                                    te.inc(mm)

                            @blk.vector
                            def _(e, g=g, pr=pr, te0=te0, gd0=gd0):
                                e.wait_ge(te.sem, te0 + NCI)
                                for j2 in range(2):
                                    t = pr * 2 + j2
                                    for ob in range(4):
                                        m = j2 * 4 + ob
                                        if m >= 2:
                                            e.wait_ge(gd.sem, gd0 + 16 * (m - 1))
                                        ve.inc(e.scalar_tensor_tensor(
                                            otmp[m % 2][:],
                                            pd[m][:, 0:512],
                                            alj[:, 8 + t:8 + t + 1],
                                            v1[t][:, ob * 512:ob * 512 + 512],
                                            ALU.mult, ALU.add))

                            @blk.gpsimd
                            def _(e, g=g, pr=pr, ve0=ve0):
                                for j2 in range(2):
                                    t = pr * 2 + j2
                                    row = (g * 4 + t) * 128
                                    for ob in range(4):
                                        m = j2 * 4 + ob
                                        e.wait_ge(ve.sem, ve0 + m + 1)
                                        gd.inc(e.dma_start(
                                            out=out[row:row + 128,
                                                    ob * 512:ob * 512 + 512],
                                            in_=otmp[m % 2][:]), 16)

                        barrier()
                sE.close()

        @blk.gpsimd
        def _(e):
            e.wait_ge(gd.sem, gd.n)

    return nc



def _strips(w, n):
    o, c = w.shape
    outp = np.zeros((n, 128, c), np.float32)
    for s in range(n):
        lo = s * 128
        hi = min(o, lo + 128)
        if hi > lo:
            outp[s, 0:hi - lo] = np.asarray(w[lo:hi], np.float32)
    return outp


def kernel(L0_attn_x, L0_attn_wq, L0_attn_wk, L0_attn_wv, L0_attn_wo,
           L0_attn_gq, L0_attn_gk, L0_attn_gv, L0_attn_go,
           L0_ffn_w_gate, L0_ffn_w_up, L0_ffn_w_down):
    f32 = np.float32
    x = np.ascontiguousarray(np.asarray(L0_attn_x, f32))

    wpad_g = _strips(np.asarray(L0_ffn_w_gate, f32), 52)
    wpad_u = _strips(np.asarray(L0_ffn_w_up, f32), 52)
    wd_p = np.zeros((2048, IP), f32)
    wd_p[:, 0:IREAL] = np.asarray(L0_ffn_w_down, f32)
    wpad_d = _strips(wd_p, 16)
    ag1 = [_strips(np.asarray(w, f32), 16)
           for w in (L0_attn_wq, L0_attn_wk, L0_attn_wv, L0_attn_wo)]

    srcA, srcB, waccs = [], [], []
    a2 = np.concatenate([wpad_g, wpad_u], axis=0)
    ag1s = np.concatenate(ag1, axis=0)
    for r in range(8):
        sa = np.concatenate([ag1s[8 * r:8 * r + 8], a2[13 * r:13 * r + 13]], 0)
        sb = wpad_d[2 * r:2 * r + 2]
        srcA.append(np.ascontiguousarray(sa))
        srcB.append(np.ascontiguousarray(sb))
        wa = np.zeros((23, 8), f32)
        for i in range(8):
            wa[i, (8 * r + i) // 16] = 1.0
        for i in range(13):
            wa[8 + i, 4 + (13 * r + i) // 52] = 1.0
        wa[21, 6] = 1.0
        wa[22, 6] = 1.0
        waccs.append(wa)

    eyefn = np.eye(128, dtype=f32)
    eyebn = np.eye(128, dtype=f32).astype(BF16NP)
    onescf = np.ones((128, 1), f32)
    onesrf = np.ones((1, 128), f32)
    ehs = np.zeros((128, 8 * 128), f32)
    for jj in range(8):
        ehs[:, jj * 128 + jj] = 1.0
    ehs = ehs.astype(BF16NP)
    ninv = np.array([1.0 / n for n in NUMEL] + [1.0], f32)[0:8].reshape(8, 1)

    nc1 = build_wsum()
    in1 = [dict(wsrcA=srcA[r], wsrcB=srcB[r], wacc=waccs[r], eyef=eyefn,
                onescf=onescf) for r in range(8)]
    res1 = run_bass_kernel_spmd(nc1, in1, core_ids=list(range(8)))
    wsums_all = np.concatenate(
        [np.asarray(res1.results[r]["psum8"], f32) for r in range(8)], axis=0)

    nc2 = build_wquant()
    in2 = [dict(wsrcA=srcA[r], wsrcB=srcB[r],
                wacct=np.ascontiguousarray(waccs[r].T),
                wsums=wsums_all, ninv=ninv, eyeb=eyebn, onescf=onescf,
                onesrf=onesrf) for r in range(8)]
    res2 = run_bass_kernel_spmd(nc2, in2, core_ids=list(range(8)))

    shA = [np.asarray(res2.results[r]["oshardA"]) for r in range(8)]
    shB = [np.asarray(res2.results[r]["oshardB"]) for r in range(8)]
    t_ag1 = np.concatenate([s[0:8] for s in shA], 0)
    t_a2 = np.concatenate([s[8:21] for s in shA], 0)
    t_b = np.concatenate(shB, 0)

    def rhs_layout(strips, width):
        s, p, ncf = strips.shape
        ncc = ncf // 128
        spb = width // 128
        a = strips.reshape(s // spb, spb, p, ncc, 128)
        a = np.ascontiguousarray(a.transpose(0, 2, 3, 1, 4))
        return a.reshape(s // spb, p, ncc * width)

    twq_h = np.ascontiguousarray(t_ag1[0:16])
    twk_h = np.ascontiguousarray(t_ag1[16:32])
    twv_h = rhs_layout(t_ag1[32:48], 512)
    two_h = rhs_layout(t_ag1[48:64], 512)
    twg_h = rhs_layout(t_a2[0:52], 256)
    twu_h = rhs_layout(t_a2[52:104], 256)
    b4 = t_b.reshape(16, 128, 52, 128)
    twd_h = np.ascontiguousarray(b4.transpose(2, 1, 0, 3)).reshape(52, 128, 2048)

    nc3 = build_main()
    in3 = []
    for r in range(8):
        b = r // 2
        xb = x[b]
        if r % 2 == 1:
            xb = np.concatenate([xb[2048:], xb[:2048]], axis=0)
        in3.append(dict(
            x=np.ascontiguousarray(xb), twq=twq_h, twk=twk_h, twv=twv_h,
            two=two_h, twg=twg_h, twu=twu_h, twd=twd_h,
            wsums=wsums_all, ninvr=np.ascontiguousarray(ninv.reshape(1, 8)),
            gq=np.asarray(L0_attn_gq, f32), gk=np.asarray(L0_attn_gk, f32),
            gv=np.asarray(L0_attn_gv, f32), go=np.asarray(L0_attn_go, f32),
            eyeb=eyebn, eyef=eyefn, onescf=onescf, onesrf=onesrf, ehs8=ehs))
    res3 = run_bass_kernel_spmd(nc3, in3, core_ids=list(range(8)))

    outp = np.empty((4, TB, D), np.float32)
    for b in range(4):
        outp[b, 0:2048] = np.asarray(res3.results[2 * b]["out"], np.float32)
        outp[b, 2048:] = np.asarray(res3.results[2 * b + 1]["out"], np.float32)
    return outp
